# version 01 (final)
# baseline (speedup 1.0000x reference)
import numpy as np
import concourse.bass as bass
import concourse.mybir as mybir
from concourse.bass_utils import run_bass_kernel_spmd

F32 = mybir.dt.float32
BF16 = mybir.dt.bfloat16
I32 = mybir.dt.int32
U8 = mybir.dt.uint8
AF = mybir.ActivationFunctionType
ALU = mybir.AluOpType
AX = mybir.AxisListType.X

T = 2048
C = 1024
FI = 3072
CT = C // 128
TT = T // 128
NCH = T // 512
IT = FI // 128
Q15 = 32767.0
EPSQ = 1e-12
LNEPS = 1e-5
AGN = 128 * FI + 128 * FI + (FI // 8) * C
DEBUG_TAPS = ()
STOP_AFTER = None
ARENA = 211456


class Sem:
    def __init__(self, h):
        self.h = h
        self.n = 0


class Prog:
    def __init__(self):
        self.ops = {k: [] for k in ("sync", "vector", "scalar", "tensor", "gpsimd")}

    def emit(self, eng, fn):
        self.ops[eng].append(fn)

    def then(self, eng, sem, make, k=1):
        sem.n += k
        kk = k
        self.ops[eng].append(lambda e: make(e).then_inc(sem.h, kk))
        return sem.n

    def mark_last(self, eng, sem, k=1):
        sem.n += k
        f = self.ops[eng].pop()
        kk = k
        self.ops[eng].append(lambda e: f(e).then_inc(sem.h, kk))
        return sem.n

    def wait(self, eng, sem, val=None):
        v = sem.n if val is None else val
        if v > 0:
            self.ops[eng].append(lambda e: e.wait_ge(sem.h, v))


class _Stop(Exception):
    pass


def build_nc(debug_taps=(), stop_after=None):
    nc = bass.Bass("TRN2", target_bir_lowering=False, debug=False,
                   detect_race_conditions=False)

    xT = nc.dram_tensor("xT", [C, T], F32, kind="ExternalInput").ap()
    wqT = nc.dram_tensor("wqT", [C, C], F32, kind="ExternalInput").ap()
    wkT = nc.dram_tensor("wkT", [C, C], F32, kind="ExternalInput").ap()
    wvT = nc.dram_tensor("wvT", [C, C], F32, kind="ExternalInput").ap()
    woT = nc.dram_tensor("woT", [C, C], F32, kind="ExternalInput").ap()
    bq = nc.dram_tensor("bq", [C], F32, kind="ExternalInput").ap()
    bk = nc.dram_tensor("bk", [C], F32, kind="ExternalInput").ap()
    bv = nc.dram_tensor("bv", [C], F32, kind="ExternalInput").ap()
    bo = nc.dram_tensor("bo", [C], F32, kind="ExternalInput").ap()
    lng = nc.dram_tensor("lng", [C], F32, kind="ExternalInput").ap()
    lnb = nc.dram_tensor("lnb", [C], F32, kind="ExternalInput").ap()
    wup_s = nc.dram_tensor("wup_s", [FI, 128], F32, kind="ExternalInput").ap()
    wg_s = nc.dram_tensor("wg_s", [FI, 128], F32, kind="ExternalInput").ap()
    wdn_s = nc.dram_tensor("wdn_s", [C, FI // 8], F32, kind="ExternalInput").ap()
    consts = nc.dram_tensor("consts", [128, 1376], F32, kind="ExternalInput").ap()
    yT = nc.dram_tensor("yT", [C, T], F32, kind="ExternalOutput").ap()

    ag_in = nc.dram_tensor("ag_in", [AGN], BF16)
    ag_out = nc.dram_tensor("ag_out", [8, AGN], BF16)
    cc_at_in = nc.dram_tensor("cc_at_in", [128, 8 * 65], F32)
    cc_at_out = nc.dram_tensor("cc_at_out", [128, 8 * 65], F32)
    am_in = [nc.dram_tensor(f"am_in{i}", [256 if i == 2 else 128], F32) for i in range(4)]
    am_out = [nc.dram_tensor(f"am_out{i}", [256 if i == 2 else 128], F32) for i in range(4)]
    sp_pg = nc.dram_tensor("sp_pg", [FI, T], BF16)
    sp_pu = nc.dram_tensor("sp_pu", [FI, T], BF16)
    sp_g2 = nc.dram_tensor("sp_g2", [FI, T], I32)
    sp_op = nc.dram_tensor("sp_op", [FI, T], BF16)
    sp_xln = nc.dram_tensor("sp_xln", [C, T], F32)
    dbg = {}
    dbg_names = list(debug_taps)

    import contextlib
    es = contextlib.ExitStack()
    with es:
        arena = es.enter_context(nc.sbuf_tensor("arena", [128, ARENA], U8))

        def vw(off, n, dt):
            sz = {F32: 4, BF16: 2, I32: 4}[dt]
            assert off + n * sz <= ARENA, (off, n, sz)
            return arena[:, off:off + n * sz].bitcast(dt)

        K = 1024
        x_bf = vw(64 * K, CT * T, BF16)
        ek_bf = vw(96 * K, TT * C, BF16)
        v_bf = vw(128 * K, TT * C, BF16)
        qe_bf = vw(128 * K, CT * T, BF16)
        wk_bf = vw(0, CT * C, BF16)
        wv_bf = vw(16 * K, CT * C, BF16)
        wq_bf = vw(0, CT * C, BF16)
        ln1_bf = vw(0, CT * T, BF16)
        wo_bf = vw(32 * K, CT * C, BF16)
        mixr = [vw(160 * K + g * 2 * K, 512, F32) for g in range(CT)]
        resid = vw(64 * K, CT * T, F32)
        rx_bf = vw(64 * K, CT * T, BF16)
        slab_gd = vw(0, 24 * K, BF16)
        slab_up = vw(128 * K, 24 * K, BF16)
        ro_ring = vw(96 * K, IT * 512, BF16)
        xln2 = vw(128 * K, CT * T, F32)
        SCR = 176 * K
        FSC = 48 * K
        deq_src = [vw(SCR, 512, F32), vw(SCR + 2 * K, 512, F32)]
        deq_sc = vw(SCR + 4 * K, 64, F32)
        deq_inv = vw(SCR + 4 * K + 256, 64, F32)
        deqi = vw(SCR + 5 * K, 512, I32)
        dqout = vw(SCR + 7 * K, 512, BF16)
        dqtr = vw(SCR + 8 * K, 128, BF16)
        bkb = vw(SCR + 9 * K, 1024, BF16)
        bvb = vw(SCR + 11 * K, 1024, BF16)
        sel8_b = vw(SCR + 9 * K, 1024, BF16)
        lnbf_ring = [vw(SCR + 12 * K, 512, BF16), vw(SCR + 13 * K, 512, BF16)]
        lnm2_ring = [vw(SCR + 14 * K, 512, BF16), vw(SCR + 15 * K, 512, BF16)]
        xst_ring = [vw(SCR + 8 * K, 512, F32), vw(SCR + 10 * K, 512, F32)]
        aosc = vw(SCR + 0 * K, 512, F32)
        pstg = [vw(FSC + i * K, 512, BF16) for i in range(4)]
        bsl = []
        for i in range(2):
            b0 = SCR + i * 7 * K
            bsl.append(dict(rg=vw(b0, 512, I32), gf=vw(b0 + 2 * K, 512, F32),
                            sg=vw(b0 + 4 * K, 512, F32), pgv=vw(b0 + 6 * K, 512, BF16)))
        csl = []
        for i in range(2):
            b0 = FSC + 0 + i * 7 * K
            csl.append(dict(pu=vw(b0, 512, BF16), g2=vw(b0 + 1 * K, 512, I32),
                            ru=vw(b0 + 3 * K, 512, I32), rg2=vw(b0 + 5 * K, 512, I32)))
        dsl = [dict(ob=vw(SCR + 14 * K + i * 0, 512, BF16), ro=None) for i in range(2)]
        dsl = [dict(ob=vw(FSC + 8 * K + i * 3 * K, 512, BF16),
                    ro=vw(FSC + 8 * K + i * 3 * K + 1 * K, 512, I32)) for i in range(2)]
        ffsc = vw(FSC + 14 * K, 512, F32)
        ln3bf_ring = [vw(FSC + 4 * K, 512, BF16), vw(FSC + 5 * K, 512, BF16)]
        ln3m2_ring = [vw(FSC + 6 * K, 512, BF16), vw(FSC + 7 * K, 512, BF16)]
        M = 192 * K
        consts_bf = vw(M, 1344, BF16)
        blkones_bf = consts_bf[:, 1024:1152]
        ones_col_bf = consts_bf[:, 1152:1153]
        ones_row_bf = consts_bf[0:1, 1153:1281]
        ones_f = vw(M + 2816, 128, F32)
        vecs = vw(M + 3328, 32, F32)
        bq_sb = vecs[:, 0:8]
        bo_sb = vecs[:, 8:16]
        lng_sb = vecs[:, 16:24]
        lnb_sb = vecs[:, 24:32]
        pack = vw(M + 3584, 520, F32)
        state_bf = vw(M + 5696, CT * 128, BF16)
        srow_bf = vw(M + 5696, CT, BF16)
        state_bf = vw(M + 9856, CT * 128, BF16)

        stat = vw(M + 5760, 1024, F32)
        rq_f = stat[:, 0:512]
        muT0 = vw(SCR + 0 * K, 512, F32)
        varT0 = vw(SCR + 2 * K, 512, F32)
        rbT0 = vw(SCR + 4 * K, 512, F32)
        bbT0 = vw(SCR + 6 * K, 512, F32)
        ln3_tiles = (vw(0, 512, F32), vw(2 * K, 512, F32),
                     vw(4 * K, 512, F32), vw(6 * K, 512, F32))
        tmp1 = vw(M + 11904, 512, F32)
        scal = vw(M + 13952, 160, F32)
        assert M + 13952 + 640 <= ARENA
        tmp1_b = tmp1.bitcast(BF16)
        rqhi_b = tmp1_b[0:16, 0:512]
        rqlo_b = tmp1_b[0:16, 512:1024]
        zq_sb = stat[0:16, 512:1024]

        amx = scal

        tap_regions = {
            "pack": (M + 3584, 2080),
            "state": (M + 9856, 2048),
            "ek": (96 * K, 4096),
            "v": (128 * K, 4096),
            "qe": (128 * K, 4096),
            "mix0": (160 * K, 2048),
            "mix3": (160 * K + 3 * 2048, 2048),
            "ln1": (0, 32768),
            "resid": (64 * K, 8192),
            "resid_pre": (64 * K, 8192),
            "ln1b": (0, 8192),
            "scal": (M + 13952, 640),
            "rx": (64 * K, 4096),
            "x2": (128 * K, 8192),
            "op": (48 * K, 2048),
        }

        psA = es.enter_context(nc.psum_tensor("psA", [128, 512], F32))
        psB = es.enter_context(nc.psum_tensor("psB", [128, 512], F32))
        psC = es.enter_context(nc.psum_tensor("psC", [128, 512], F32))
        psD = es.enter_context(nc.psum_tensor("psD", [128, 512], F32))
        psU0 = es.enter_context(nc.psum_tensor("psU0", [128, 512], F32))
        psU1 = es.enter_context(nc.psum_tensor("psU1", [128, 512], F32))
        psS = es.enter_context(nc.psum_tensor("psS", [128, 512], F32))
        psX = es.enter_context(nc.psum_tensor("psX", [128, 512], F32))
        psMM = [psA, psB, psC, psD]

        sd = Sem(es.enter_context(nc.semaphore("sd")))
        sp = Sem(es.enter_context(nc.semaphore("sp")))
        xs2 = [Sem(es.enter_context(nc.semaphore("xs0"))),
               Sem(es.enter_context(nc.semaphore("xs1")))]
        ds2 = [Sem(es.enter_context(nc.semaphore("ds0"))),
               Sem(es.enter_context(nc.semaphore("ds1")))]
        gd = Sem(es.enter_context(nc.semaphore("gd")))
        amd = Sem(es.enter_context(nc.semaphore("amd")))
        gq = Sem(es.enter_context(nc.semaphore("gq")))
        gsl = Sem(es.enter_context(nc.semaphore("gsl")))
        gar = Sem(es.enter_context(nc.semaphore("gar")))
        spf = [Sem(es.enter_context(nc.semaphore(f"spf{i}"))) for i in range(4)]
        ld2 = [Sem(es.enter_context(nc.semaphore(f"ld2{i}"))) for i in range(2)]
        fsp = [Sem(es.enter_context(nc.semaphore(f"fsp{i}"))) for i in range(2)]
        tsm = Sem(es.enter_context(nc.semaphore("ts")))
        vsm = Sem(es.enter_context(nc.semaphore("vs")))
        ssm = Sem(es.enter_context(nc.semaphore("ss")))
        gsm = Sem(es.enter_context(nc.semaphore("gs")))
        ccm = Sem(es.enter_context(nc.semaphore("cc")))

        p = Prog()
        x_src = xT.rearrange("(g p) t -> p g t", p=128)

        for n2 in dbg_names:
            off, nbytes = tap_regions[n2]
            dbg[n2] = nc.dram_tensor("dbg_" + n2, [128, nbytes], U8,
                                     kind="ExternalOutput").ap()

        def tap(name):
            if name in dbg:
                off, nbytes = tap_regions[name]
                p.wait("sync", vsm)
                p.wait("sync", ssm)
                p.then("sync", sp, lambda s, off=off, nb=nbytes, n=name:
                       s.dma_start(out=dbg[n][:, 0:nb],
                                   in_=arena[:, off:off + nb]), 16)

        def maybe_stop(tag):
            if stop_after is not None and tag == stop_after:
                spill_map = {"gate": sp_pg, "upb": sp_g2, "c": sp_op}
                if tag in spill_map:
                    for sm in (sp, fsp[0], fsp[1], spf[0], spf[1], spf[2], spf[3]):
                        p.wait("gpsimd", sm, sm.n)
                    spl = spill_map[tag]
                    eo = p.then("gpsimd", gd, lambda g, spl=spl: g.dma_start(
                        out=yT.rearrange("(g p) t -> p g t", p=128),
                        in_=spl.ap()[0:1024, :].rearrange("(g p) t -> p g t", p=128)), 16)
                    p.wait("gpsimd", gd, eo)
                    raise _Stop()
                eo = p.then("sync", sp, lambda s: s.dma_start(
                    out=yT.rearrange("(g p) t -> p g t", p=128),
                    in_=vw(64 * K, CT * T, F32).rearrange("p (g t) -> p g t", g=CT)), 16)
                p.wait("gpsimd", sp, eo)
                raise _Stop()

        try:
            p.emit("gpsimd", lambda g: g.memset(scal[:, :], 0.0))
            p.then("gpsimd", gd, lambda g: g.dma_start(
                out=x_bf.rearrange("p (g t) -> p g t", g=CT), in_=x_src), 16)
            p.then("gpsimd", gd, lambda g: g.dma_start(
                out=wk_bf.rearrange("p (g o) -> p g o", g=CT),
                in_=wkT.rearrange("(g p) o -> p g o", p=128)), 16)
            p.then("gpsimd", gd, lambda g: g.dma_start(
                out=wv_bf.rearrange("p (g o) -> p g o", g=CT),
                in_=wvT.rearrange("(g p) o -> p g o", p=128)), 16)
            p.then("gpsimd", gd, lambda g: g.dma_start(
                out=bkb[0:1, :], in_=bk.rearrange("(o c) -> o c", o=1)), 16)
            p.then("gpsimd", gd, lambda g: g.dma_start(
                out=bvb[0:1, :], in_=bv.rearrange("(o c) -> o c", o=1)), 16)
            e_gload = p.then("gpsimd", gd, lambda g: g.dma_start(
                out=consts_bf[:, :], in_=consts[:, 0:1344]), 16)
            p.then("sync", sp, lambda s: s.dma_start(
                out=ones_f[0:1, :], in_=consts[0:1, 1153:1281]), 16)
            e_sload = p.then("sync", sp, lambda s: s.dma_start(
                out=vecs[:, :], in_=consts[:, 1344:1376]), 16)
            p.wait("gpsimd", gd, e_gload)
            p.wait("gpsimd", sp, e_sload)
            maybe_stop("p0")

            p.wait("tensor", gd, e_gload)
            p.wait("tensor", sp, e_sload)

            def xtile(g, tt):
                return x_bf[:, g * T + tt * 128: g * T + tt * 128 + 128]

            ek_done = {}
            v_done = {}
            for tt in range(TT):
                pk = [psMM[0], psMM[1]] if tt % 2 == 0 else [psU0, psU1]
                pv = [psMM[2], psMM[3]] if tt % 2 == 0 else [psS, psX]
                if tt >= 2:
                    p.wait("tensor", ssm, ek_done[tt - 2])
                    p.wait("tensor", ssm, v_done[tt - 2])
                for ch in range(2):
                    p.emit("tensor", lambda t, pk=pk, ch=ch: t.matmul(
                        pk[ch][:, :], ones_row_bf, bkb[0:1, ch * 512:(ch + 1) * 512],
                        start=True, stop=False))
                    p.emit("tensor", lambda t, pv=pv, ch=ch: t.matmul(
                        pv[ch][:, :], ones_row_bf, bvb[0:1, ch * 512:(ch + 1) * 512],
                        start=True, stop=False))
                for g in range(CT):
                    last = g == CT - 1
                    for ch in range(2):
                        p.emit("tensor", lambda t, pk=pk, g=g, tt=tt, ch=ch, last=last:
                               t.matmul(pk[ch][:, :], xtile(g, tt),
                                        wk_bf[:, g * C + ch * 512: g * C + (ch + 1) * 512],
                                        start=False, stop=last))
                        p.emit("tensor", lambda t, pv=pv, g=g, tt=tt, ch=ch, last=last:
                               t.matmul(pv[ch][:, :], xtile(g, tt),
                                        wv_bf[:, g * C + ch * 512: g * C + (ch + 1) * 512],
                                        start=False, stop=last))
                et = p.mark_last("tensor", tsm)
                p.wait("scalar", tsm, et)
                for ch in range(2):
                    p.emit("scalar", lambda s, pk=pk, tt=tt, ch=ch: s.activation(
                        out=ek_bf[:, tt * C + ch * 512: tt * C + (ch + 1) * 512],
                        in_=pk[ch][:, :], func=AF.Exp))
                ek_done[tt] = p.then("scalar", ssm, lambda s: s.drain())
                for ch in range(2):
                    p.emit("scalar", lambda s, pv=pv, tt=tt, ch=ch: s.copy(
                        out=v_bf[:, tt * C + ch * 512: tt * C + (ch + 1) * 512],
                        in_=pv[ch][:, :]))
                v_done[tt] = p.then("scalar", ssm, lambda s: s.drain())
                maybe_stop(f"kv{tt}")
            e_kv_s = ek_done[TT - 1]
            e_kv_v = v_done[TT - 1]
            maybe_stop("kv")

            deq_free = {}

            def ag_v(eoff, O, CSL, c2, rt):
                region = ag_in.ap()[eoff: eoff + CSL * O].rearrange(
                    "(p o) -> p o", p=CSL)
                return region[c2 * 128:(c2 + 1) * 128, rt * 128:(rt + 1) * 128]

            def dequant(src_ap, O, CSL, eoff, key):
                for rt in range(O // 128):
                    buf = rt % 2
                    st = deq_src[buf][:, 0:CSL]
                    if rt >= 2:
                        p.wait("sync", vsm, deq_free[(key, rt - 2)])
                    ld = p.then("sync", sd, lambda s, st=st, rt=rt, src=src_ap:
                                s.dma_start(out=st, in_=src[rt * 128:(rt + 1) * 128, :]), 16)
                    p.wait("vector", sd, ld)
                    nb = CSL // 32
                    sc = deq_sc[:, buf * 16: buf * 16 + nb]
                    inv = deq_inv[:, buf * 16: buf * 16 + nb]
                    for b in range(nb):
                        p.emit("vector", lambda v, st=st, sc=sc, b=b: v.tensor_reduce(
                            out=sc[:, b:b + 1], in_=st[:, b * 32:(b + 1) * 32],
                            axis=AX, op=ALU.max, apply_absolute_value=True))
                    p.emit("vector", lambda v: v.drain())
                    p.emit("vector", lambda v, sc=sc, nb=nb: v.tensor_scalar(
                        out=sc[:, 0:nb], in0=sc[:, 0:nb], scalar1=1.0 / 7.0,
                        scalar2=None, op0=ALU.mult))
                    p.emit("vector", lambda v: v.drain())
                    p.emit("vector", lambda v, sc=sc, nb=nb: v.tensor_scalar(
                        out=sc[:, 0:nb], in0=sc[:, 0:nb], scalar1=EPSQ,
                        scalar2=None, op0=ALU.max))
                    p.emit("vector", lambda v: v.drain())
                    p.emit("vector", lambda v, sc=sc, inv=inv, nb=nb: v.reciprocal(
                        out=inv[:, 0:nb], in_=sc[:, 0:nb]))
                    p.emit("vector", lambda v: v.drain())
                    for b in range(nb):
                        p.emit("vector", lambda v, st=st, inv=inv, b=b: v.tensor_scalar(
                            out=deqi[:, b * 32:(b + 1) * 32],
                            in0=st[:, b * 32:(b + 1) * 32],
                            scalar1=inv[:, b:b + 1], scalar2=None, op0=ALU.mult))
                    p.emit("vector", lambda v: v.drain())
                    for b in range(nb):
                        p.emit("vector", lambda v, sc=sc, b=b: v.tensor_scalar(
                            out=dqout[:, b * 32:(b + 1) * 32],
                            in0=deqi[:, b * 32:(b + 1) * 32],
                            scalar1=sc[:, b:b + 1], scalar2=None, op0=ALU.mult))
                    deq_free[(key, rt)] = p.then("vector", vsm, lambda v: v.drain())
                    p.wait("sync", vsm, deq_free[(key, rt)])
                    for c2 in range(CSL // 128):
                        tr = p.then("sync", sd, lambda s, c2=c2: s.dma_start(
                            out=dqtr[:, :], in_=dqout[:, c2 * 128:(c2 + 1) * 128],
                            transpose=True), 16)
                        p.wait("sync", sd, tr)
                        p.then("sync", sd, lambda s, c2=c2, rt=rt, eo=eoff, O=O, CSL=CSL:
                               s.dma_start(out=ag_v(eo, O, CSL, c2, rt), in_=dqtr[:, :]), 16)

            dequant(wup_s, FI, 128, 0, "up")
            dequant(wg_s, FI, 128, 128 * FI, "g")
            dequant(wdn_s, C, FI // 8, 2 * 128 * FI, "dn")
            e_deq = sd.n
            p.wait("gpsimd", tsm, tsm.n)
            e_sel = p.then("gpsimd", gsl, lambda g: g.dma_start(
                out=sel8_b[:, :], in_=consts[:, 0:1024]), 16)

            p.wait("tensor", ssm, e_kv_s)
            p.wait("tensor", ssm, e_kv_v)
            for g in range(CT):
                pU = psU0 if g < 4 else psU1
                uc = (g % 4) * 128
                for tt in range(TT):
                    p.emit("tensor", lambda t, pU=pU, uc=uc, g=g, tt=tt: t.matmul(
                        pU[:, uc:uc + 128],
                        ek_bf[:, tt * C + g * 128: tt * C + g * 128 + 128],
                        v_bf[:, tt * C + g * 128: tt * C + g * 128 + 128],
                        start=(tt == 0), stop=(tt == TT - 1)))
                    p.emit("tensor", lambda t, g=g, tt=tt: t.matmul(
                        psS[:, 384 + g: 385 + g],
                        ek_bf[:, tt * C + g * 128: tt * C + g * 128 + 128],
                        ones_col_bf, start=(tt == 0), stop=(tt == TT - 1)))
            e_umm = p.mark_last("tensor", tsm)
            p.wait("scalar", tsm, e_umm)
            for g in range(CT):
                pU = psU0 if g < 4 else psU1
                uc = (g % 4) * 128
                p.emit("scalar", lambda s, pU=pU, uc=uc, g=g: s.copy(
                    out=pack[0:64, g * 65: g * 65 + 64], in_=pU[0:64, uc: uc + 64]))
                p.emit("scalar", lambda s, pU=pU, uc=uc, g=g: s.copy(
                    out=pack[64:128, g * 65: g * 65 + 64],
                    in_=pU[64:128, uc + 64: uc + 128]))
                p.emit("scalar", lambda s, g=g: s.copy(
                    out=pack[:, g * 65 + 64: g * 65 + 65], in_=psS[:, 384 + g: 385 + g]))
            e_pack = p.then("scalar", ssm, lambda s: s.drain())
            p.wait("sync", ssm, e_pack)
            e_packd = p.then("sync", sp, lambda s: s.dma_start(
                out=cc_at_in.ap(), in_=pack[:, :]), 16)
            p.wait("gpsimd", sp, e_packd)
            e_arat = p.then("gpsimd", ccm, lambda g: g.collective_compute(
                "AllReduce", ALU.add, replica_groups=[[0, 1], [2, 3], [4, 5], [6, 7]],
                ins=[cc_at_in.ap().opt()], outs=[cc_at_out.ap().opt()]))
            p.wait("gpsimd", sd, e_deq)
            p.wait("gpsimd", ccm, e_arat)
            e_ag = p.then("gpsimd", ccm, lambda g: g.collective_compute(
                "AllGather", ALU.bypass, replica_groups=[list(range(8))],
                ins=[ag_in.ap().opt()], outs=[ag_out.ap().opt()]))

            maybe_stop("ar_issued")
            p.then("gpsimd", gq, lambda g: g.dma_start(
                out=wq_bf.rearrange("p (g o) -> p g o", g=CT),
                in_=wqT.rearrange("(g p) o -> p g o", p=128)), 16)
            e_wq = gq.n
            p.wait("tensor", gq, e_wq)
            qe_done = {}
            for ot in range(CT):
                if ot >= 1:
                    p.wait("tensor", ssm, qe_done[ot - 1])
                for g in range(CT):
                    for tc in range(NCH):
                        p.emit("tensor", lambda t, g=g, ot=ot, tc=tc: t.matmul(
                            psMM[tc][:, :],
                            wq_bf[:, g * C + ot * 128: g * C + ot * 128 + 128],
                            x_bf[:, g * T + tc * 512: g * T + (tc + 1) * 512],
                            start=(g == 0), stop=(g == CT - 1)))
                eq = p.mark_last("tensor", tsm)
                p.wait("scalar", tsm, eq)
                for tc in range(NCH):
                    p.emit("scalar", lambda s, ot=ot, tc=tc: s.activation(
                        out=qe_bf[:, ot * T + tc * 512: ot * T + (tc + 1) * 512],
                        in_=psMM[tc][:, :], func=AF.Exp, bias=bq_sb[:, ot:ot + 1]))
                qe_done[ot] = p.then("scalar", ssm, lambda s: s.drain())
            p.wait("tensor", ssm, qe_done[CT - 1])
            p.wait("tensor", ssm, e_pack)
            p.wait("tensor", gsl, e_sel)
            for tc in range(NCH):
                for g in range(CT):
                    p.emit("tensor", lambda t, g=g, tc=tc: t.matmul(
                        psS[0:16, :], blkones_bf[:, g * 16:(g + 1) * 16],
                        qe_bf[:, g * T + tc * 512: g * T + (tc + 1) * 512],
                        start=(g == 0), stop=(g == CT - 1)))
                ez = p.mark_last("tensor", tsm)
                p.wait("scalar", tsm, ez)
                p.emit("scalar", lambda s: s.copy(out=zq_sb, in_=psS[0:16, :]))
                ezc = p.then("scalar", ssm, lambda s: s.drain())
                p.wait("vector", ssm, ezc)
                p.emit("vector", lambda v: v.reciprocal(out=rq_f[0:16, :], in_=zq_sb))
                p.emit("vector", lambda v: v.drain())
                p.emit("vector", lambda v: v.tensor_copy(out=rqhi_b, in_=rq_f[0:16, :]))
                p.emit("vector", lambda v: v.drain())
                p.emit("vector", lambda v: v.tensor_copy(out=zq_sb, in_=rqhi_b))
                p.emit("vector", lambda v: v.drain())
                p.emit("vector", lambda v: v.tensor_tensor(
                    out=zq_sb, in0=rq_f[0:16, :], in1=zq_sb, op=ALU.subtract))
                p.emit("vector", lambda v: v.drain())
                p.emit("vector", lambda v: v.tensor_copy(out=rqlo_b, in_=zq_sb))
                erq = p.then("vector", vsm, lambda v: v.drain())
                p.wait("tensor", vsm, erq)
                for g in range(CT):
                    p.emit("tensor", lambda t, g=g: t.matmul(
                        psX[:, :], sel8_b[0:16, g * 128:(g + 1) * 128], rqhi_b,
                        start=True, stop=False))
                    p.emit("tensor", lambda t, g=g: t.matmul(
                        psX[:, :], sel8_b[0:16, g * 128:(g + 1) * 128], rqlo_b,
                        start=False, stop=True))
                    eb = p.mark_last("tensor", tsm)
                    p.wait("scalar", tsm, eb)
                    p.emit("scalar", lambda s: s.copy(out=aosc[:, :], in_=psX[:, :]))
                    ebc = p.then("scalar", ssm, lambda s: s.drain())
                    p.wait("vector", ssm, ebc)
                    p.emit("vector", lambda v, g=g, tc=tc: v.tensor_tensor(
                        out=qe_bf[:, g * T + tc * 512: g * T + (tc + 1) * 512],
                        in0=qe_bf[:, g * T + tc * 512: g * T + (tc + 1) * 512],
                        in1=aosc[:, :], op=ALU.mult))
                    ev = p.then("vector", vsm, lambda v: v.drain())
                    p.wait("tensor", vsm, ev)
                    p.wait("scalar", vsm, ev)

            e_zq = tsm.n
            tap("qe")
            maybe_stop("qzq")

            p.wait("gpsimd", ccm, e_arat)
            e_arb = p.then("gpsimd", gar, lambda g: g.dma_start(
                out=pack[:, :], in_=cc_at_out.ap()), 16)
            p.emit("gpsimd", lambda g: g.memset(state_bf[:, :], 0.0))
            e_st0 = p.then("gpsimd", gsm, lambda g: g.drain())
            p.wait("vector", gar, e_arb)
            p.wait("vector", gsm, e_st0)
            rzkv = deq_sc[:, 0:8]
            for g in range(CT):
                p.emit("vector", lambda v, g=g: v.reciprocal(
                    out=rzkv[:, g:g + 1], in_=pack[:, g * 65 + 64: g * 65 + 65]))
            p.emit("vector", lambda v: v.drain())
            for g in range(CT):
                p.emit("vector", lambda v, g=g: v.tensor_scalar(
                    out=state_bf[0:64, g * 128: g * 128 + 64],
                    in0=pack[0:64, g * 65: g * 65 + 64],
                    scalar1=rzkv[0:64, g:g + 1], scalar2=None, op0=ALU.mult))
                p.emit("vector", lambda v, g=g: v.tensor_scalar(
                    out=state_bf[64:128, g * 128 + 64: g * 128 + 128],
                    in0=pack[64:128, g * 65: g * 65 + 64],
                    scalar1=rzkv[64:128, g:g + 1], scalar2=None, op0=ALU.mult))
            p.emit("vector", lambda v: v.drain())
            for g in range(CT):
                def _srow(v, g=g):
                    with nc.allow_low_precision(reason="srow is a bf16 matmul operand"):
                        v.tensor_reduce(out=srow_bf[:, g:g + 1],
                                        in_=state_bf[:, g * 128:(g + 1) * 128],
                                        axis=AX, op=ALU.add)
                p.emit("vector", _srow)
            e_state = p.then("vector", vsm, lambda v: v.drain())
            tap("pack")
            tap("state")
            p.wait("tensor", vsm, e_state)

            def ln_chunk(tc, sum_emitters, apply_src, out_emit, tiles=None):
                muT, varT, rbT, bbT = tiles if tiles is not None else (
                    muT0, varT0, rbT0, bbT0)
                muT_b = muT.bitcast(BF16)
                sum_emitters()
                ee = tsm.n
                p.wait("scalar", tsm, ee)
                sA = stat[0:1, 0:512]
                sB = stat[0:1, 512:1024]
                hiA = tmp1_b[0:1, 0:512]
                hiB = tmp1_b[0:1, 512:1024]
                loA = muT_b[0:1, 0:512]
                loB = muT_b[0:1, 512:1024]
                rsA = varT[0:1, 0:512]
                p.emit("scalar", lambda s: s.copy(out=sA, in_=psS[0:1, :]))
                p.emit("scalar", lambda s: s.copy(out=hiA, in_=psS[0:1, :]))
                p.emit("scalar", lambda s: s.copy(out=sB, in_=psX[32:33, :]))
                p.emit("scalar", lambda s: s.copy(out=hiB, in_=psX[32:33, :]))
                evs = p.then("scalar", ssm, lambda s: s.drain())
                p.wait("vector", ssm, evs)
                p.emit("vector", lambda v: v.tensor_copy(out=rsA, in_=hiA))
                p.emit("vector", lambda v: v.drain())
                p.emit("vector", lambda v: v.tensor_tensor(
                    out=loA, in0=sA, in1=rsA, op=ALU.subtract))
                p.emit("vector", lambda v: v.drain())
                p.emit("vector", lambda v: v.tensor_copy(out=rsA, in_=hiB))
                p.emit("vector", lambda v: v.drain())
                p.emit("vector", lambda v: v.tensor_tensor(
                    out=loB, in0=sB, in1=rsA, op=ALU.subtract))
                evv = p.then("vector", vsm, lambda v: v.drain())
                p.wait("tensor", vsm, evv)
                p.emit("tensor", lambda t: t.matmul(
                    psX[:, :], ones_row_bf, hiA, start=True, stop=False))
                p.emit("tensor", lambda t: t.matmul(
                    psX[:, :], ones_row_bf, loA, start=False, stop=True))
                p.emit("tensor", lambda t: t.matmul(
                    psU1[:, :], ones_row_bf, hiB, start=True, stop=False))
                p.emit("tensor", lambda t: t.matmul(
                    psU1[:, :], ones_row_bf, loB, start=False, stop=True))
                eb_ = p.mark_last("tensor", tsm)
                p.wait("scalar", tsm, eb_)
                p.emit("scalar", lambda s: s.activation(
                    out=muT[:, :], in_=psX[:, :], func=AF.Identity, scale=1.0 / C))
                p.emit("scalar", lambda s: s.activation(
                    out=varT[:, :], in_=psU1[:, :], func=AF.Identity, scale=1.0 / C))
                emv = p.then("scalar", ssm, lambda s: s.drain())
                p.wait("vector", ssm, emv)
                p.emit("vector", lambda v: v.tensor_tensor(
                    out=tmp1[:, :], in0=muT[:, :], in1=muT[:, :], op=ALU.mult))
                p.emit("vector", lambda v: v.tensor_tensor(
                    out=varT[:, :], in0=varT[:, :], in1=tmp1[:, :], op=ALU.subtract))
                p.emit("vector", lambda v: v.tensor_scalar(
                    out=varT[:, :], in0=varT[:, :], scalar1=LNEPS, scalar2=None,
                    op0=ALU.add))
                ev_ = p.then("vector", vsm, lambda v: v.drain())
                p.wait("scalar", vsm, ev_)
                p.emit("scalar", lambda s: s.activation(
                    out=tmp1[:, :], in_=varT[:, :], func=AF.Sqrt))
                es2 = p.then("scalar", ssm, lambda s: s.drain())
                p.wait("vector", ssm, es2)
                p.emit("vector", lambda v: v.reciprocal(out=rbT[:, :], in_=tmp1[:, :]))
                p.emit("vector", lambda v: v.tensor_tensor(
                    out=tmp1[:, :], in0=rbT[:, :], in1=rbT[:, :], op=ALU.mult))
                p.emit("vector", lambda v: v.tensor_tensor(
                    out=tmp1[:, :], in0=tmp1[:, :], in1=varT[:, :], op=ALU.mult))
                p.emit("vector", lambda v: v.tensor_scalar(
                    out=tmp1[:, :], in0=tmp1[:, :], scalar1=-0.5, scalar2=None,
                    op0=ALU.mult))
                p.emit("vector", lambda v: v.tensor_scalar(
                    out=tmp1[:, :], in0=tmp1[:, :], scalar1=1.5, scalar2=None,
                    op0=ALU.add))
                p.emit("vector", lambda v: v.tensor_tensor(
                    out=rbT[:, :], in0=rbT[:, :], in1=tmp1[:, :], op=ALU.mult))
                p.emit("vector", lambda v: v.tensor_tensor(
                    out=bbT[:, :], in0=muT[:, :], in1=rbT[:, :], op=ALU.mult))
                p.emit("vector", lambda v: v.tensor_scalar(
                    out=bbT[:, :], in0=bbT[:, :], scalar1=-1.0, scalar2=None,
                    op0=ALU.mult))
                for g in range(CT):
                    src2 = apply_src(g)
                    p.emit("vector", lambda v, src2=src2: v.tensor_tensor(
                        out=tmp1[:, :], in0=src2, in1=rbT[:, :], op=ALU.mult))
                    p.emit("vector", lambda v: v.tensor_tensor(
                        out=tmp1[:, :], in0=tmp1[:, :], in1=bbT[:, :], op=ALU.add))
                    ev3 = p.then("vector", vsm, lambda v: v.drain())
                    p.wait("scalar", vsm, ev3)
                    out_emit(g)
                    es3 = p.then("scalar", ssm, lambda s: s.drain())
                    p.wait("vector", ssm, es3)
                e_ap = p.then("scalar", ssm, lambda s: s.drain())
                p.wait("tensor", ssm, e_ap)

            for tc in range(NCH):
                mdone = {}
                for g in range(CT):
                    if g >= 4:
                        p.wait("tensor", ssm, mdone[g - 4])
                    p.emit("tensor", lambda t, g=g, tc=tc: t.matmul(
                        psMM[g % 4][:, :], state_bf[:, g * 128:(g + 1) * 128],
                        qe_bf[:, g * T + tc * 512: g * T + (tc + 1) * 512],
                        start=True, stop=True))
                    em = p.mark_last("tensor", tsm)
                    p.wait("scalar", tsm, em)
                    p.emit("scalar", lambda s, g=g: s.copy(
                        out=mixr[g][:, :], in_=psMM[g % 4][:, :]))
                    mdone[g] = p.then("scalar", ssm, lambda s: s.drain())

                def ln1_sums(tc=tc, mdone=mdone):
                    for g in range(CT):
                        p.emit("tensor", lambda t, g=g, tc=tc: t.matmul(
                            psS[0:1, :], srow_bf[:, g:g + 1],
                            qe_bf[:, g * T + tc * 512: g * T + (tc + 1) * 512],
                            start=(g == 0), stop=(g == CT - 1)))
                    sq_ev = {}
                    for g in range(CT):
                        slot = lnm2_ring[g % 2]
                        p.wait("scalar", ssm, mdone[g])
                        if g >= 2:
                            p.wait("scalar", tsm, sq_ev[g - 2])
                        p.emit("scalar", lambda s, g=g, slot=slot: s.activation(
                            out=slot[:, :], in_=mixr[g][:, :], func=AF.Square))
                        esq = p.then("scalar", ssm, lambda s: s.drain())
                        p.wait("tensor", ssm, esq)
                        p.emit("tensor", lambda t, slot=slot, g=g: t.matmul(
                            psX[32:33, :], ones_col_bf, slot[:, :],
                            start=(g == 0), stop=(g == CT - 1)))
                        sq_ev[g] = p.mark_last("tensor", tsm)

                def ln1_out(g, tc=tc):
                    p.emit("scalar", lambda s, g=g, tc=tc: s.activation(
                        out=ln1_bf[:, g * T + tc * 512: g * T + (tc + 1) * 512],
                        in_=tmp1[:, :], func=AF.Identity,
                        scale=lng_sb[:, g:g + 1], bias=lnb_sb[:, g:g + 1]))

                ln_chunk(tc, ln1_sums, lambda g: mixr[g][:, :], ln1_out)
            tap("ln1")
            tap("mix0")
            maybe_stop("ln1")
            tap("mix3")

            p.wait("gpsimd", ssm, ssm.n)
            p.then("gpsimd", gq, lambda g: g.dma_start(
                out=wo_bf.rearrange("p (g o) -> p g o", g=CT),
                in_=woT.rearrange("(g p) o -> p g o", p=128)), 16)
            e_wo = gq.n
            p.wait("tensor", gq, e_wo)
            wo_done = {}
            xld = {}
            for ot in range(CT):
                if ot >= 1:
                    p.wait("tensor", vsm, wo_done[ot - 1])
                for g in range(CT):
                    for tc in range(NCH):
                        p.emit("tensor", lambda t, g=g, ot=ot, tc=tc: t.matmul(
                            psMM[tc][:, :],
                            wo_bf[:, g * C + ot * 128: g * C + ot * 128 + 128],
                            ln1_bf[:, g * T + tc * 512: g * T + (tc + 1) * 512],
                            start=(g == 0), stop=(g == CT - 1)))
                ew = p.mark_last("tensor", tsm)
                p.wait("scalar", tsm, ew)
                for tc in range(NCH):
                    k = ot * NCH + tc
                    xs = xst_ring[k % 2]
                    xsem = xs2[k % 2]
                    if k >= 2:
                        p.wait("sync", vsm, xld[k - 2])
                    if k == 0:
                        p.wait("sync", sd, e_deq)
                        p.wait("sync", tsm, e_zq)
                    exl = p.then("sync", xsem, lambda s, xs=xs, ot=ot, tc=tc: s.dma_start(
                        out=xs[:, :],
                        in_=xT[ot * 128:(ot + 1) * 128, tc * 512:(tc + 1) * 512]), 16)
                    p.emit("scalar", lambda s, tc=tc, ot=ot: s.activation(
                        out=aosc[:, :], in_=psMM[tc][:, :], func=AF.Identity,
                        bias=bo_sb[:, ot:ot + 1]))
                    es4 = p.then("scalar", ssm, lambda s: s.drain())
                    p.wait("vector", ssm, es4)
                    p.wait("vector", xsem, exl)
                    p.emit("vector", lambda v, xs=xs, ot=ot, tc=tc: v.tensor_tensor(
                        out=resid[:, ot * T + tc * 512: ot * T + (tc + 1) * 512],
                        in0=xs[:, :], in1=aosc[:, :], op=ALU.add))
                    xld[k] = p.then("vector", vsm, lambda v: v.drain())
                    p.wait("scalar", vsm, xld[k])
                wo_done[ot] = p.then("vector", vsm, lambda v: v.drain())
            e_womm = tsm.n
            p.wait("tensor", vsm, wo_done[CT - 1])
            tap("resid_pre")

            def make_ln_sums_streamed(src_f32, bf_ring, m2_ring, tc):
                def sums():
                    cp_ev = {}
                    for g in range(CT):
                        bslot = bf_ring[g % 2]
                        mslot = m2_ring[g % 2]
                        if g >= 2:
                            p.wait("vector", tsm, cp_ev[g - 2][0])
                            p.wait("scalar", tsm, cp_ev[g - 2][1])
                        p.emit("vector", lambda v, g=g, bslot=bslot: v.tensor_copy(
                            out=bslot[:, :],
                            in_=src_f32[:, g * T + tc * 512: g * T + (tc + 1) * 512]))
                        evb = p.then("vector", vsm, lambda v: v.drain())
                        p.emit("scalar", lambda s, g=g, mslot=mslot: s.activation(
                            out=mslot[:, :],
                            in_=src_f32[:, g * T + tc * 512: g * T + (tc + 1) * 512],
                            func=AF.Square))
                        esb = p.then("scalar", ssm, lambda s: s.drain())
                        p.wait("tensor", vsm, evb)
                        p.wait("tensor", ssm, esb)
                        p.emit("tensor", lambda t, bslot=bslot, g=g: t.matmul(
                            psS[0:1, :], ones_col_bf, bslot[:, :],
                            start=(g == 0), stop=(g == CT - 1)))
                        p.emit("tensor", lambda t, mslot=mslot, g=g: t.matmul(
                            psX[32:33, :], ones_col_bf, mslot[:, :],
                            start=(g == 0), stop=(g == CT - 1)))
                        e_mm2 = p.mark_last("tensor", tsm)
                        cp_ev[g] = (e_mm2, e_mm2)
                return sums

            for tc in range(NCH):
                def ln2_out(g, tc=tc):
                    p.emit("scalar", lambda s, g=g, tc=tc: s.copy(
                        out=resid[:, g * T + tc * 512: g * T + (tc + 1) * 512],
                        in_=tmp1[:, :]))

                ln_chunk(tc, make_ln_sums_streamed(resid, lnbf_ring, lnm2_ring, tc),
                         lambda g, tc=tc: resid[:, g * T + tc * 512: g * T + (tc + 1) * 512],
                         ln2_out)
            xln = resid
            tap("resid")
            maybe_stop("ln2")

            AMX_X, AMX_T, AMX_G, AMX_U, AMX_G2, AMX_O = 148, 149, 150, 151, 152, 153

            AMP_X, AMP_G, AMP_U, AMP_G2, AMP_O = 0, 32, 128, 224, 320

            def amax_part(src, pc):
                p.emit("vector", lambda v, src=src, pc=pc: v.tensor_reduce(
                    out=stat[:, pc:pc + 1], in_=src, axis=AX, op=ALU.max,
                    apply_absolute_value=True))

            def amax_fin(base, n, slot):
                p.emit("vector", lambda v: v.drain())
                p.emit("vector", lambda v, base=base, n=n, slot=slot:
                       v.tensor_reduce(out=amx[:, slot:slot + 1],
                                       in_=stat[:, base:base + n], axis=AX,
                                       op=ALU.max))
                p.emit("vector", lambda v: v.drain())

            for i2 in range(CT * NCH):
                amax_part(xln[:, i2 * 512:(i2 + 1) * 512], AMP_X + i2)
            amax_fin(AMP_X, CT * NCH, AMX_X)
            e_am0 = p.then("vector", vsm, lambda v: v.drain())
            p.wait("sync", vsm, e_am0)
            e_am0d = p.then("sync", amd, lambda s: s.dma_start(
                out=am_in[0].ap()[0:128].rearrange("(p one) -> p one", one=1),
                in_=amx[:, AMX_X:AMX_X + 1]), 16)
            p.wait("gpsimd", amd, e_am0d)
            p.wait("gpsimd", ccm, e_ag)
            e_ar0 = p.then("gpsimd", ccm, lambda g: g.collective_compute(
                "AllReduce", ALU.max, replica_groups=[list(range(8))],
                ins=[am_in[0].ap().opt()], outs=[am_out[0].ap().opt()]))
            e_xlnsp = p.then("sync", sp, lambda s: s.dma_start(
                out=sp_xln.ap().rearrange("(g p) t -> p g t", p=128),
                in_=xln.rearrange("p (g t) -> p g t", g=CT)), 16)
            p.wait("gpsimd", ccm, e_ar0)
            e_am0r = p.then("gpsimd", gar, lambda g: g.dma_start(
                out=scal[0:1, 0:128],
                in_=am_out[0].ap()[0:128].rearrange("(one p) -> one p", one=1)), 16)

            def vdrain():
                p.emit("vector", lambda v: v.drain())

            def sred(dst, src_row=0):
                p.emit("vector", lambda v, dst=dst, sr=src_row: v.tensor_reduce(
                    out=scal[0:1, dst:dst + 1], in_=scal[sr:sr + 1, 0:128],
                    axis=AX, op=ALU.max))
                vdrain()

            def s_from_a(s_col, pre=None):
                if pre is not None:
                    p.emit("vector", lambda v, sc=s_col, pre=pre: v.tensor_tensor(
                        out=scal[0:1, sc:sc + 1], in0=scal[0:1, sc:sc + 1],
                        in1=scal[0:1, pre:pre + 1], op=ALU.mult))
                    vdrain()
                p.emit("vector", lambda v, sc=s_col: v.tensor_scalar(
                    out=scal[0:1, sc:sc + 1], in0=scal[0:1, sc:sc + 1],
                    scalar1=1.0 / Q15, scalar2=None, op0=ALU.mult))
                vdrain()
                p.emit("vector", lambda v, sc=s_col: v.tensor_scalar(
                    out=scal[0:1, sc:sc + 1], in0=scal[0:1, sc:sc + 1],
                    scalar1=EPSQ, scalar2=None, op0=ALU.max))
                vdrain()

            def recipc(dst, src):
                p.emit("vector", lambda v, d=dst, s2=src: v.reciprocal(
                    out=scal[0:1, d:d + 1], in_=scal[0:1, s2:s2 + 1]))
                vdrain()

            def mulc(dst, a, b):
                p.emit("vector", lambda v, d=dst, a=a, b=b: v.tensor_tensor(
                    out=scal[0:1, d:d + 1], in0=scal[0:1, a:a + 1],
                    in1=scal[0:1, b:b + 1], op=ALU.mult))
                vdrain()

            bc_tb = scal[0:1, 156:160].bitcast(BF16)
            bc_r1 = scal[0:1, 154:155]
            bc_r2 = scal[0:1, 155:156]

            bc_prev = {"eb5": 0, "cp": 0}

            def bcast(src_col, dst_col):
                sc = src_col
                p.wait("vector", tsm, bc_prev["eb5"])
                p.emit("vector", lambda v: v.tensor_copy(
                    out=bc_tb[0:1, 0:1], in_=scal[0:1, sc:sc + 1]))
                vdrain()
                p.emit("vector", lambda v: v.tensor_copy(
                    out=bc_r1, in_=bc_tb[0:1, 0:1]))
                vdrain()
                p.emit("vector", lambda v: v.tensor_tensor(
                    out=bc_r1, in0=scal[0:1, sc:sc + 1], in1=bc_r1,
                    op=ALU.subtract))
                vdrain()
                p.emit("vector", lambda v: v.tensor_copy(
                    out=bc_tb[0:1, 1:2], in_=bc_r1))
                vdrain()
                p.emit("vector", lambda v: v.tensor_copy(
                    out=bc_r2, in_=bc_tb[0:1, 1:2]))
                vdrain()
                p.emit("vector", lambda v: v.tensor_tensor(
                    out=bc_r2, in0=bc_r1, in1=bc_r2, op=ALU.subtract))
                vdrain()
                p.emit("vector", lambda v: v.tensor_copy(
                    out=bc_tb[0:1, 2:3], in_=bc_r2))
                vprev = p.then("vector", vsm, lambda v: v.drain())
                p.wait("tensor", vsm, vprev)
                p.wait("tensor", ssm, bc_prev["cp"])
                for i3 in range(3):
                    p.emit("tensor", lambda t, i3=i3: t.matmul(
                        psS[:, 508:509], ones_row_bf, bc_tb[0:1, i3:i3 + 1],
                        start=(i3 == 0), stop=(i3 == 2)))
                eb5 = p.mark_last("tensor", tsm)
                bc_prev["eb5"] = eb5
                p.wait("scalar", tsm, eb5)
                p.emit("scalar", lambda s, dc=dst_col: s.copy(
                    out=scal[:, dc:dc + 1], in_=psS[:, 508:509]))
                cp = p.then("scalar", ssm, lambda s: s.drain())
                bc_prev["cp"] = cp
                return cp

            p.wait("vector", gar, e_am0r)
            sred(128)
            s_from_a(128)
            recipc(129, 128)
            e_k1b = bcast(129, 140)

            p.wait("vector", ssm, e_k1b)
            p.wait("vector", sp, e_xlnsp)
            rx_ev = {}
            for g in range(CT):
                for tc in range(NCH):
                    k2 = g * NCH + tc
                    if k2 >= 1:
                        p.wait("vector", vsm, rx_ev[k2 - 1])
                    p.emit("vector", lambda v, g=g, tc=tc: v.tensor_scalar(
                        out=deqi[:, :], in0=xln[:, g * T + tc * 512: g * T + (tc + 1) * 512],
                        scalar1=scal[:, 140:141], scalar2=None, op0=ALU.mult))
                    p.emit("vector", lambda v, g=g, tc=tc: v.tensor_copy(
                        out=rx_bf[:, g * T + tc * 512: g * T + (tc + 1) * 512],
                        in_=deqi[:, :]))
                    rx_ev[k2] = p.then("vector", vsm, lambda v: v.drain())
            e_rx = rx_ev[CT * NCH - 1]
            tap("rx")
            maybe_stop("rx")

            def w_g_row(g):
                return ag_out.ap()[g, 128 * FI: 2 * 128 * FI].rearrange("(p o) -> p o", p=128)

            def w_up_row(g):
                return ag_out.ap()[g, 0: 128 * FI].rearrange("(p o) -> p o", p=128)

            def w_dn_row(it):
                r = it // 3
                off = 2 * 128 * FI + (it % 3) * 128 * C
                return ag_out.ap()[r, off: off + 128 * C].rearrange("(p o) -> p o", p=128)

            p.wait("sync", ccm, e_ag)
            p.wait("sync", tsm, e_womm)
            gslab_ev = 0
            for g in range(CT):
                gslab_ev = p.then("sync", sd, lambda s, g=g: s.dma_start(
                    out=slab_gd[:, g * 3072:(g + 1) * 3072], in_=w_g_row(g)), 16)

            def ffn_proj(slab, spill, amax_slot, wait_slab_ev, pbase):
                evk = {}
                stg_dma = {}
                stg_vs = {}
                p.wait("tensor", sd, wait_slab_ev)
                for it in range(IT):
                    for tc in range(NCH):
                        k2 = it * NCH + tc
                        pb = psMM[k2 % 4]
                        if k2 >= 4:
                            p.wait("tensor", ssm, evk[k2 - 4])
                        for g in range(CT):
                            p.emit("tensor",
                                   lambda t, pb=pb, slab=slab, it=it, g=g, tc=tc: t.matmul(
                                       pb[:, :],
                                       slab[:, g * 3072 + it * 128: g * 3072 + (it + 1) * 128],
                                       rx_bf[:, g * T + tc * 512: g * T + (tc + 1) * 512],
                                       start=(g == 0), stop=(g == CT - 1)))
                        ee = p.mark_last("tensor", tsm)
                        stg = pstg[k2 % 4]
                        p.wait("scalar", tsm, ee)
                        if k2 >= 4:
                            p.wait("scalar", spf[k2 % 4], stg_dma[k2 - 4])
                            p.wait("scalar", vsm, stg_vs[k2 - 4])
                        p.emit("scalar", lambda s, pb=pb, stg=stg: s.copy(
                            out=stg[:, :], in_=pb[:, :]))
                        evk[k2] = p.then("scalar", ssm, lambda s: s.drain())
                        p.wait("vector", ssm, evk[k2])
                        amax_part(stg[:, :], pbase + k2)
                        stg_vs[k2] = p.then("vector", vsm, lambda v: v.drain())
                        p.wait("sync", ssm, evk[k2])
                        stg_dma[k2] = p.then("sync", spf[k2 % 4],
                                             lambda s, stg=stg, it=it, tc=tc, spill=spill:
                                             s.dma_start(
                            out=spill.ap()[it * 128:(it + 1) * 128, tc * 512:(tc + 1) * 512],
                            in_=stg[:, :]), 16)
                return evk[IT * NCH - 1]

            p.wait("tensor", vsm, e_rx)
            e_gmm = ffn_proj(slab_gd, sp_pg, AMX_G, gslab_ev, AMP_G)
            maybe_stop("gate")
            amax_fin(AMP_G, IT * NCH, AMX_G)
            e_amg = p.then("vector", vsm, lambda v: v.drain())
            p.wait("sync", vsm, e_amg)
            e_am1d = p.then("sync", amd, lambda s: s.dma_start(
                out=am_in[1].ap()[0:128].rearrange("(p one) -> p one", one=1),
                in_=amx[:, AMX_G:AMX_G + 1]), 16)
            p.wait("gpsimd", amd, e_am1d)
            e_ar1 = p.then("gpsimd", ccm, lambda g: g.collective_compute(
                "AllReduce", ALU.max, replica_groups=[list(range(8))],
                ins=[am_in[1].ap().opt()], outs=[am_out[1].ap().opt()]))
            p.wait("gpsimd", ccm, e_ar1)
            e_am1r = p.then("gpsimd", gar, lambda g: g.dma_start(
                out=scal[0:1, 0:128],
                in_=am_out[1].ap()[0:128].rearrange("(one p) -> one p", one=1)), 16)
            p.wait("vector", gar, e_am1r)
            sred(130)
            s_from_a(130, pre=128)
            recipc(131, 130)
            mulc(131, 131, 128)
            bcast(131, 141)
            e_sgb = bcast(130, 142)

            uslab_ev = 0
            for g in range(CT):
                uslab_ev = p.then("sync", sd, lambda s, g=g: s.dma_start(
                    out=slab_up[:, g * 3072:(g + 1) * 3072], in_=w_up_row(g)), 16)
            e_umm2 = ffn_proj(slab_up, sp_pu, AMX_U, uslab_ev, AMP_U)

            p.wait("vector", ssm, e_sgb)
            bfree = {}
            for it in range(IT):
                for tc in range(NCH):
                    k2 = it * NCH + tc
                    sl = bsl[k2 % 2]
                    if k2 >= 2:
                        p.wait("sync", fsp[k2 % 2], bfree[k2 - 2])
                    eld = p.then("sync", ld2[k2 % 2], lambda s, sl=sl, it=it, tc=tc: s.dma_start(
                        out=sl["pgv"][:, :], in_=sp_pg.ap()[it * 128:(it + 1) * 128,
                                                            tc * 512:(tc + 1) * 512]), 16)
                    p.wait("vector", ld2[k2 % 2], eld)
                    p.emit("vector", lambda v, sl=sl: v.tensor_scalar(
                        out=sl["rg"][:, :], in0=sl["pgv"][:, :], scalar1=scal[:, 141:142],
                        scalar2=None, op0=ALU.mult))
                    p.emit("vector", lambda v, sl=sl: v.tensor_scalar(
                        out=sl["gf"][:, :], in0=sl["rg"][:, :], scalar1=scal[:, 142:143],
                        scalar2=None, op0=ALU.mult))
                    ev7 = p.then("vector", vsm, lambda v: v.drain())
                    p.wait("scalar", vsm, ev7)
                    p.emit("scalar", lambda s, sl=sl: s.activation(
                        out=sl["sg"][:, :], in_=sl["gf"][:, :], func=AF.Sigmoid))
                    es7 = p.then("scalar", ssm, lambda s: s.drain())
                    p.wait("vector", ssm, es7)
                    sii = sl["gf"].bitcast(I32)
                    p.emit("vector", lambda v, sl=sl, sii=sii: v.tensor_scalar(
                        out=sii[:, :], in0=sl["sg"][:, :], scalar1=65536.0,
                        scalar2=None, op0=ALU.mult))
                    p.emit("vector", lambda v, sii=sii: v.tensor_scalar(
                        out=sii[:, :], in0=sii[:, :], scalar1=65535, scalar2=None,
                        op0=ALU.min))
                    g2o = sl["sg"].bitcast(I32)
                    p.emit("vector", lambda v, sl=sl, sii=sii, g2o=g2o: v.tensor_tensor(
                        out=g2o[:, :], in0=sl["rg"][:, :], in1=sii[:, :], op=ALU.mult))
                    amax_part(g2o[:, :], AMP_G2 + (k2 % 96))
                    ev9 = p.then("vector", vsm, lambda v: v.drain())
                    p.wait("sync", vsm, ev9)
                    bfree[k2] = p.then("sync", fsp[k2 % 2], lambda s, g2o=g2o, it=it, tc=tc:
                                       s.dma_start(
                        out=sp_g2.ap()[it * 128:(it + 1) * 128, tc * 512:(tc + 1) * 512],
                        in_=g2o[:, :]), 16)
            e_b = p.then("vector", vsm, lambda v: v.drain())
            maybe_stop("upb")

            amax_fin(AMP_U, IT * NCH, AMX_U)
            amax_fin(AMP_G2, 96, AMX_G2)
            e_b2 = p.then("vector", vsm, lambda v: v.drain())
            p.wait("sync", vsm, e_b2)
            p.then("sync", amd, lambda s: s.dma_start(
                out=am_in[2].ap()[0:128].rearrange("(p one) -> p one", one=1),
                in_=amx[:, AMX_U:AMX_U + 1]), 16)
            e_am2d = p.then("sync", amd, lambda s: s.dma_start(
                out=am_in[2].ap()[128:256].rearrange("(p one) -> p one", one=1),
                in_=amx[:, AMX_G2:AMX_G2 + 1]), 16)
            p.wait("gpsimd", amd, e_am2d)
            e_ar2 = p.then("gpsimd", ccm, lambda g: g.collective_compute(
                "AllReduce", ALU.max, replica_groups=[list(range(8))],
                ins=[am_in[2].ap().opt()], outs=[am_out[2].ap().opt()]))
            p.wait("gpsimd", ccm, e_ar2)
            p.then("gpsimd", gar, lambda g: g.dma_start(
                out=scal[0:1, 0:128],
                in_=am_out[2].ap()[0:128].rearrange("(one p) -> one p", one=1)), 16)
            e_am2r = p.then("gpsimd", gar, lambda g: g.dma_start(
                out=scal[32:33, 0:128],
                in_=am_out[2].ap()[128:256].rearrange("(one p) -> one p", one=1)), 16)
            p.wait("vector", gar, e_am2r)
            sred(132)
            s_from_a(132, pre=128)
            recipc(133, 132)
            mulc(133, 133, 128)
            p.emit("vector", lambda v: v.tensor_scalar(
                out=scal[0:1, 138:139], in0=scal[0:1, 130:131],
                scalar1=1.0 / 65536.0, scalar2=None, op0=ALU.mult))
            sred(134, src_row=32)
            s_from_a(134, pre=138)
            recipc(135, 134)
            mulc(135, 135, 138)
            bcast(133, 143)
            e_kg2b = bcast(135, 144)

            p.wait("vector", ssm, e_kg2b)
            cfree = {}
            for it in range(IT):
                for tc in range(NCH):
                    k2 = it * NCH + tc
                    sl = csl[k2 % 2]
                    if k2 >= 2:
                        p.wait("sync", fsp[k2 % 2], cfree[k2 - 2])
                    p.then("sync", ld2[k2 % 2], lambda s, sl=sl, it=it, tc=tc: s.dma_start(
                        out=sl["pu"][:, :], in_=sp_pu.ap()[it * 128:(it + 1) * 128,
                                                           tc * 512:(tc + 1) * 512]), 16)
                    eld2 = p.then("sync", ld2[k2 % 2], lambda s, sl=sl, it=it, tc=tc: s.dma_start(
                        out=sl["g2"][:, :], in_=sp_g2.ap()[it * 128:(it + 1) * 128,
                                                           tc * 512:(tc + 1) * 512]), 16)
                    p.wait("vector", ld2[k2 % 2], eld2)
                    p.emit("vector", lambda v, sl=sl: v.tensor_scalar(
                        out=sl["ru"][:, :], in0=sl["pu"][:, :], scalar1=scal[:, 143:144],
                        scalar2=None, op0=ALU.mult))
                    p.emit("vector", lambda v, sl=sl: v.tensor_scalar(
                        out=sl["rg2"][:, :], in0=sl["g2"][:, :], scalar1=scal[:, 144:145],
                        scalar2=None, op0=ALU.mult))
                    opb = sl["pu"]
                    p.emit("vector", lambda v, sl=sl, opb=opb: v.tensor_tensor(
                        out=opb[:, :], in0=sl["rg2"][:, :], in1=sl["ru"][:, :],
                        op=ALU.mult))
                    amax_part(opb[:, :], AMP_O + (k2 % 96))
                    ev10 = p.then("vector", vsm, lambda v: v.drain())
                    p.wait("sync", vsm, ev10)
                    cfree[k2] = p.then("sync", fsp[k2 % 2], lambda s, opb=opb, it=it, tc=tc:
                                       s.dma_start(
                        out=sp_op.ap()[it * 128:(it + 1) * 128, tc * 512:(tc + 1) * 512],
                        in_=opb[:, :]), 16)
            amax_fin(AMP_O, 96, AMX_O)
            e_c = p.then("vector", vsm, lambda v: v.drain())
            maybe_stop("c")
            p.wait("sync", vsm, e_c)
            e_am3d = p.then("sync", amd, lambda s: s.dma_start(
                out=am_in[3].ap()[0:128].rearrange("(p one) -> p one", one=1),
                in_=amx[:, AMX_O:AMX_O + 1]), 16)
            p.wait("gpsimd", amd, e_am3d)
            e_ar3 = p.then("gpsimd", ccm, lambda g: g.collective_compute(
                "AllReduce", ALU.max, replica_groups=[list(range(8))],
                ins=[am_in[3].ap().opt()], outs=[am_out[3].ap().opt()]))
            p.wait("gpsimd", ccm, e_ar3)
            e_am3r = p.then("gpsimd", gar, lambda g: g.dma_start(
                out=scal[0:1, 0:128],
                in_=am_out[3].ap()[0:128].rearrange("(one p) -> one p", one=1)), 16)
            p.wait("vector", gar, e_am3r)
            mulc(139, 134, 132)
            sred(136)
            s_from_a(136, pre=139)
            recipc(137, 136)
            mulc(137, 137, 139)
            bcast(137, 145)
            e_sob = bcast(136, 146)
            tap("scal")

            dslab_ev = 0
            for it in range(IT):
                dslab_ev = p.then("sync", sd, lambda s, it=it: s.dma_start(
                    out=slab_gd[:, it * 1024:(it + 1) * 1024], in_=w_dn_row(it)), 16)
            p.wait("sync", sd, dslab_ev)
            p.wait("sync", ssm, e_umm2)
            e_xln2 = p.then("sync", sp, lambda s: s.dma_start(
                out=xln2.rearrange("p (g t) -> p g t", g=CT),
                in_=sp_xln.ap().rearrange("(g p) t -> p g t", p=128)), 16)

            p.wait("vector", ssm, e_sob)
            p.wait("scalar", ssm, e_sob)
            pall = [psA, psB, psC, psD, psU0, psU1, psS, psX]
            dn_done = {}
            edn_prev = 0
            prev_ro = None
            for tc in range(NCH):
                ro_ev = {}
                if tc >= 1:
                    p.wait("vector", tsm, edn_prev)
                for it in range(IT):
                    k2 = it % 2
                    sl = dsl[k2]
                    dsem = ds2[it % 2]
                    if it >= 2:
                        p.wait("sync", vsm, ro_ev[it - 2])
                    elif prev_ro is not None:
                        p.wait("sync", vsm, prev_ro[IT - 2 + it])
                    eld3 = p.then("sync", dsem, lambda s, sl=sl, it=it, tc=tc: s.dma_start(
                        out=sl["ob"][:, :], in_=sp_op.ap()[it * 128:(it + 1) * 128,
                                                           tc * 512:(tc + 1) * 512]), 16)
                    p.wait("vector", dsem, eld3)
                    p.emit("vector", lambda v, sl=sl: v.tensor_scalar(
                        out=sl["ro"][:, :], in0=sl["ob"][:, :], scalar1=scal[:, 145:146],
                        scalar2=None, op0=ALU.mult))
                    p.emit("vector", lambda v, sl=sl, it=it: v.tensor_copy(
                        out=ro_ring[:, it * 512:(it + 1) * 512], in_=sl["ro"][:, :]))
                    ro_ev[it] = p.then("vector", vsm, lambda v: v.drain())
                prev_ro = ro_ev
                p.wait("tensor", vsm, ro_ev[IT - 1])
                if tc == 0:
                    p.wait("tensor", sd, dslab_ev)
                if tc >= 1:
                    p.wait("tensor", ssm, dn_done[tc - 1])
                for it in range(IT):
                    for h in range(CT):
                        p.emit("tensor", lambda t, it=it, h=h: t.matmul(
                            pall[h][:, :],
                            slab_gd[:, it * 1024 + h * 128: it * 1024 + (h + 1) * 128],
                            ro_ring[:, it * 512:(it + 1) * 512],
                            start=(it == 0), stop=(it == IT - 1)))
                edn = p.mark_last("tensor", tsm)
                edn_prev = edn
                p.wait("scalar", tsm, edn)
                if tc == 0:
                    p.wait("scalar", sp, e_xln2)
                for h in range(CT):
                    p.emit("scalar", lambda s, h=h: s.activation(
                        out=ffsc[:, :], in_=pall[h][:, :], func=AF.Identity,
                        scale=scal[:, 146:147]))
                    es12 = p.then("scalar", ssm, lambda s: s.drain())
                    p.wait("vector", ssm, es12)
                    p.emit("vector", lambda v, h=h, tc=tc: v.tensor_tensor(
                        out=xln2[:, h * T + tc * 512: h * T + (tc + 1) * 512],
                        in0=xln2[:, h * T + tc * 512: h * T + (tc + 1) * 512],
                        in1=ffsc[:, :], op=ALU.add))
                    ev12 = p.then("vector", vsm, lambda v: v.drain())
                    p.wait("scalar", vsm, ev12)
                dn_done[tc] = p.then("scalar", ssm, lambda s: s.drain())
            p.wait("vector", ssm, dn_done[NCH - 1])
            p.wait("tensor", ssm, dn_done[NCH - 1])

            tap("x2")
            maybe_stop("down")
            out_dma = []
            for tc in range(NCH):
                def ln3_out(g, tc=tc):
                    p.emit("scalar", lambda s, g=g, tc=tc: s.copy(
                        out=xln2[:, g * T + tc * 512: g * T + (tc + 1) * 512],
                        in_=tmp1[:, :]))

                ln_chunk(tc, make_ln_sums_streamed(xln2, ln3bf_ring, ln3m2_ring, tc),
                         lambda g, tc=tc: xln2[:, g * T + tc * 512: g * T + (tc + 1) * 512],
                         ln3_out, tiles=ln3_tiles)
            p.wait("sync", ssm, ssm.n)
            e_out = p.then("sync", sp, lambda s: s.dma_start(
                out=yT.rearrange("(g p) t -> p g t", p=128),
                in_=xln2.rearrange("p (g t) -> p g t", g=CT)), 16)
            p.wait("gpsimd", sp, e_out)

        except _Stop:
            pass

        _x = None
        with nc.Block() as block:
            @block.sync
            def _(eng):
                for fn in p.ops["sync"]:
                    fn(eng)

            @block.vector
            def _(eng):
                for fn in p.ops["vector"]:
                    fn(eng)

            @block.scalar
            def _(eng):
                for fn in p.ops["scalar"]:
                    fn(eng)

            @block.tensor
            def _(eng):
                for fn in p.ops["tensor"]:
                    fn(eng)

            @block.gpsimd
            def _(eng):
                for fn in p.ops["gpsimd"]:
                    fn(eng)

    return nc


def _make_consts(bqv, bov, lngv, lnbv):
    consts = np.zeros((128, 1376), np.float32)
    for i, vv in enumerate((bqv, bov, lngv, lnbv)):
        consts[:, 1344 + 8 * i: 1352 + 8 * i] = vv.reshape(8, 128).T
    for g in range(8):
        consts[2 * g, g * 128: g * 128 + 64] = 1.0
        consts[2 * g + 1, g * 128 + 64: g * 128 + 128] = 1.0
    for g in range(8):
        consts[0:64, 1024 + g * 16 + 2 * g] = 1.0
        consts[64:128, 1024 + g * 16 + 2 * g + 1] = 1.0
    consts[:, 1152] = 1.0
    consts[0, 1153:1281] = 1.0
    return consts


_NC_CACHE = {}


def kernel(L0_attn_x, L0_attn_Wq, L0_attn_bq, L0_attn_Wk, L0_attn_bk,
           L0_attn_Wv, L0_attn_bv, L0_attn_Wo, L0_attn_bo, L0_attn_ln_g,
           L0_attn_ln_b, L0_ffn_w_gate, L0_ffn_w_up, L0_ffn_w_down):
    x = np.ascontiguousarray(np.asarray(L0_attn_x, np.float32))
    wqT = np.ascontiguousarray(np.asarray(L0_attn_Wq, np.float32).T)
    wkT = np.ascontiguousarray(np.asarray(L0_attn_Wk, np.float32).T)
    wvT = np.ascontiguousarray(np.asarray(L0_attn_Wv, np.float32).T)
    woT = np.ascontiguousarray(np.asarray(L0_attn_Wo, np.float32).T)
    wg = np.asarray(L0_ffn_w_gate, np.float32)
    wu = np.asarray(L0_ffn_w_up, np.float32)
    wd = np.asarray(L0_ffn_w_down, np.float32)
    consts = _make_consts(np.asarray(L0_attn_bq, np.float32),
                          np.asarray(L0_attn_bo, np.float32),
                          np.asarray(L0_attn_ln_g, np.float32),
                          np.asarray(L0_attn_ln_b, np.float32))

    key = ("prog", DEBUG_TAPS, STOP_AFTER)
    if key not in _NC_CACHE:
        _NC_CACHE[key] = build_nc(DEBUG_TAPS, STOP_AFTER)
    nc = _NC_CACHE[key]

    in_maps = []
    for core in range(8):
        seq, half = core // 2, core % 2
        xs = np.ascontiguousarray(x[seq, half * T:(half + 1) * T, :].T)
        in_maps.append({
            "xT": xs, "wqT": wqT, "wkT": wkT, "wvT": wvT, "woT": woT,
            "bq": np.asarray(L0_attn_bq, np.float32),
            "bk": np.asarray(L0_attn_bk, np.float32),
            "bv": np.asarray(L0_attn_bv, np.float32),
            "bo": np.asarray(L0_attn_bo, np.float32),
            "lng": np.asarray(L0_attn_ln_g, np.float32),
            "lnb": np.asarray(L0_attn_ln_b, np.float32),
            "wup_s": np.ascontiguousarray(wu[:, core * 128:(core + 1) * 128]),
            "wg_s": np.ascontiguousarray(wg[:, core * 128:(core + 1) * 128]),
            "wdn_s": np.ascontiguousarray(wd[:, core * 384:(core + 1) * 384]),
            "consts": consts,
        })
    res = run_bass_kernel_spmd(nc, in_maps, core_ids=list(range(8)))
    kernel._dbg = [{k2: v for k2, v in r.items() if k2.startswith("dbg_")}
                   for r in res.results]
    out = np.empty((4, 4096, 1024), np.float32)
    for core in range(8):
        seq, half = core // 2, core % 2
        out[seq, half * T:(half + 1) * T, :] = res.results[core]["yT"].T
    return out
